# version 04 (final)
# baseline (speedup 1.0000x reference)
import contextlib
import numpy as np
import ml_dtypes
import concourse.bass as bass
import concourse.mybir as mybir
from concourse.bass_utils import run_bass_kernel_spmd
from concourse.bass_core_rust import IndirectOffsetOnAxis

F32 = mybir.dt.float32
BF16 = mybir.dt.bfloat16
I32 = mybir.dt.int32
AF = mybir.ActivationFunctionType
ALU = mybir.AluOpType
AXX = mybir.AxisListType.X
RG8 = [[0, 1, 2, 3, 4, 5, 6, 7]]

NT = 32
DM = 1024
NH = 16
CAP = 1024
NITER = 49

LOGK = float(np.log(np.float32(1152.0)).astype(np.float32))
A1 = float((np.float32(LOGK) - np.log(np.float32(8192.0)).astype(np.float32))
           .astype(np.float32))


def _bfc(x):
    return np.ascontiguousarray(np.asarray(x, np.float32).astype(ml_dtypes.bfloat16))


def build(debug=False, upto=9, li_mode=4, dumps=None):
    if dumps is None:
        dumps = debug
    nc = bass.Bass("TRN2", target_bir_lowering=False)
    ctx = contextlib.ExitStack()

    def din(name, shape, dt=F32):
        return nc.dram_tensor(name, shape, dt, kind="ExternalInput").ap()

    x_h = din("x", [4096, DM])
    Wqkv_h = din("Wqkv", [DM, 3 * DM])
    Wg_h = din("Wg", [DM, DM])
    bg_h = din("bg", [1, DM])
    Wp_h = din("Wp", [DM, DM])
    bp_h = din("bp", [1, DM])
    rt_h = din("rt", [1, DM])
    gl_h = din("gl", [1, DM])
    w1l_h = din("w1l", [512, DM])
    b1l_h = din("b1l", [1, 512])
    w2l_h = din("w2l", [DM, 512])
    b2l_h = din("b2l", [1, DM])
    gh_h = din("gh", [1, DM])
    w1h_h = din("w1h", [4096, DM])
    b1h_h = din("b1h", [1, 4096])
    w2h_h = din("w2h", [DM, 4096])
    b2h_h = din("b2h", [1, DM])
    kvsel_h = din("kvsel", [64, 8])
    ssel_h = din("ssel", [128, 16])
    hsel_h = din("hsel", [128, 2])

    out_h = nc.dram_tensor("out", [4096, DM], F32, kind="ExternalOutput").ap()

    dbg = {}
    if debug:
        def dout(name, shape, dt=F32):
            dbg[name] = nc.dram_tensor(name, shape, dt, kind="ExternalOutput").ap()
        dout("d_kv", [64, 1040])
        dout("d_sfull", [128, 64])
        dout("d_arep", [128, 1])
        dout("d_x2_t0", [128, DM])
        dout("d_rank", [128, 64])
        dout("d_slots", [128, 32])
        dout("d_light_t0", [128, DM])
        dout("d_qp_t0", [128, DM], BF16)
        dout("d_stats", [128, 192])
        dout("d_wq", [128, 3072], BF16)
        dout("d_g_t0", [128, DM])
        dout("d_xn1T_t0", [128, DM], BF16)
        dout("d_xnf_t0", [128, DM], BF16)
        dout("d_h1l_s0", [128, 2048], BF16)
        dout("d_w1lT", [128, 4096], BF16)

    qprime_d = nc.dram_tensor("qprime_d", [4096, DM], BF16, kind="Internal").ap()
    x2_d = nc.dram_tensor("x2_d", [4096, DM], F32, kind="Internal").ap()
    light_d = nc.dram_tensor("light_d", [4096, DM], F32, kind="Internal").ap()
    cin_d = nc.dram_tensor("cin_d", [CAP + 128, DM], BF16, kind="Internal").ap()
    cout_d = nc.dram_tensor("cout_d", [CAP + 128, DM], F32, kind="Internal").ap()
    w1hT_d = nc.dram_tensor("w1hT_d", [DM, 4096], BF16, kind="Internal").ap()
    w2hT_d = nc.dram_tensor("w2hT_d", [4096, DM], BF16, kind="Internal").ap()
    kvag_in = nc.dram_tensor("kvag_in", [1, 64 * 1040], F32, kind="Internal").ap()
    kvag_out = nc.dram_tensor("kvag_out", [8, 64 * 1040], F32, kind="Internal",
                              addr_space="Shared").ap()
    sag_in = nc.dram_tensor("sag_in", [1, 4096], F32, kind="Internal").ap()
    sag_out = nc.dram_tensor("sag_out", [8, 4096], F32, kind="Internal",
                             addr_space="Shared").ap()

    ident_c = nc.inline_tensor(_bfc(np.eye(128)), "ident_c").ap()
    triu_c = nc.inline_tensor(_bfc(np.triu(np.ones((128, 128)))), "triu_c").ap()
    onesr_bf_c = nc.inline_tensor(_bfc(np.ones((1, 128))), "onesr_bf_c").ap()
    onesr_f_c = nc.inline_tensor(np.ones((1, 128), np.float32), "onesr_f_c").ap()
    negr_f_c = nc.inline_tensor(-np.ones((1, 128), np.float32), "negr_f_c").ap()
    onesc_f_c = nc.inline_tensor(np.ones((128, 1), np.float32), "onesc_f_c").ap()
    logk_cc = nc.inline_tensor(np.full((1, 1), LOGK, np.float32), "logk_cc").ap()
    capr_cc = nc.inline_tensor(
        (CAP + np.arange(128, dtype=np.float32)).reshape(128, 1), "capr_cc").ap()
    capr1_cc = nc.inline_tensor(
        (CAP + 1 + np.arange(128, dtype=np.float32)).reshape(128, 1),
        "capr1_cc").ap()

    S, V, ev = {}, {}, {}
    for name in ["xA", "xB", "qW", "qR", "x2W", "x2R", "lW", "lR", "wD", "hD",
                 "hS", "cinZ", "cinW", "cinR", "coutW", "coutZ", "hvR", "outW",
                 "kvD", "sD", "te", "ve", "se", "cc1", "cc2"]:
        S[name] = ctx.enter_context(nc.semaphore(name))
        V[name] = 0

    def inc(name, instr, n=1):
        instr.then_inc(S[name], n)
        V[name] += n
        return instr

    def w(eng, name, val=None):
        eng.wait_ge(S[name], V[name] if val is None else val)

    sb = nc.alloc_sbuf_tensor
    ident = sb("ident", [128, 128], BF16)
    triu = sb("triu", [128, 128], BF16)
    onesr_bf = sb("onesr_bf", [1, 128], BF16)
    onesr_f = sb("onesr_f", [1, 128], F32)
    negr_f = sb("negr_f", [1, 128], F32)
    onesc_f = sb("onesc_f", [128, 1], F32)
    logk_t = sb("logk_t", [1, 1], F32)
    capr = sb("capr", [128, 1], F32)
    capr1 = sb("capr1", [128, 1], F32)
    bg_r = sb("bg_r", [1, DM], BF16)
    bp_r = sb("bp_r", [1, DM], BF16)
    b2l_r = sb("b2l_r", [1, DM], BF16)
    b2h_r = sb("b2h_r", [1, DM], BF16)
    b1l_t = sb("b1l_t", [128, 4], F32)
    b1h_t = sb("b1h_t", [128, 32], F32)
    gl_t = sb("gl_t", [128, 8], F32)
    gh_t = sb("gh_t", [128, 8], F32)
    r_rep = sb("r_rep", [128, DM], F32)
    Sr = sb("Sr", [128, 1], F32)
    kvsel_t = sb("kvsel_t", [64, 8], F32)
    ssel_t = sb("ssel_t", [128, 16], F32)
    hsel_t = sb("hsel_t", [128, 2], F32)
    sums1 = sb("sums1", [128, NT], F32)
    ssq1 = sb("ssq1", [128, NT], F32)
    negm1 = sb("negm1", [128, NT], F32)
    rstd1 = sb("rstd1", [128, NT], F32)
    s_mine = sb("s_mine", [128, NT], F32)
    slots_i = sb("slots_i", [128, NT], I32)
    junk_s = sb("junk_s", [128, DM], BF16)
    junk_v = sb("junk_v", [128, DM], BF16)
    zero_f = sb("zero_f", [128, 1024], F32)
    zero_bf = sb("zero_bf", [128, 1024], BF16)
    kv_all = sb("kv_all", [64, 1040], F32)
    kvs = sb("kvs", [64, 1040], F32)
    ksum_rep = sb("ksum_rep", [128, DM], F32)
    kv_bd = sb("kv_bd", [128, 8 * 128], BF16)
    s_full = sb("s_full", [128, 64], F32)
    u64 = sb("u64", [128, 64], F32)
    u64b = sb("u64b", [128, 64], F32)
    part1 = sb("part1", [128, 1], F32)
    a_rep = sb("a_rep", [128, 1], F32)
    na_rep = sb("na_rep", [128, 1], F32)
    a_sc = sb("a_sc", [1, 1], F32)
    lntot = sb("lntot", [1, 1], F32)
    st32a = sb("st32a", [128, NT], F32)
    st32b = sb("st32b", [128, NT], F32)
    Wp_bf = sb("Wp_bf", [128, 8 * DM], BF16)
    w1lT = sb("w1lT", [128, 8 * 512], BF16)
    w2lT = sb("w2lT", [128, 4 * DM], BF16)
    flags_f = sb("flags_f", [128, 64], F32)
    flags_b = sb("flags_b", [128, 64], BF16)
    pp_sb = sb("pp_sb", [128, 64], F32)
    cs_sb = sb("cs_sb", [1, 64], F32)
    ipA = sb("ipA", [1, 64], F32)
    ipB = sb("ipB", [1, 64], F32)
    coloff = sb("coloff", [1, 64], F32)
    rank_sb = sb("rank_sb", [128, 64], F32)
    r0t = sb("r0t", [128, 1], F32)
    sc64a = sb("sc64a", [128, 64], F32)
    sc64b = sb("sc64b", [128, 64], F32)
    t1c = sb("t1c", [128, 1], F32)
    t2c = sb("t2c", [128, 1], F32)
    t3c = sb("t3c", [128, 1], F32)
    t4c = sb("t4c", [128, 1], F32)
    t5c = sb("t5c", [128, 1], F32)
    eps5 = sb("eps5", [128, 1], F32)
    eps30 = sb("eps30", [128, 1], F32)

    blk = ctx.enter_context(nc.Block())


    def _barrier():
        for engf in [blk.sync, blk.tensor, blk.vector, blk.scalar, blk.gpsimd]:
            def _bar(eng):
                for name in S:
                    if V[name] > 0:
                        eng.wait_ge(S[name], V[name])
            engf(_bar)

    def _finish(stacks):
        @blk.sync
        def _(sync):
            for name in S:
                if V[name] > 0:
                    w(sync, name)
        for st in stacks:
            st.close()
        ctx.close()
        return nc, dbg


    p01 = contextlib.ExitStack()
    Wqkv_bf = p01.enter_context(nc.sbuf_tensor("Wqkv_bf", [128, 8 * 3072], BF16))
    Wg_bf = p01.enter_context(nc.sbuf_tensor("Wg_bf", [128, 8 * DM], BF16))

    p0s = contextlib.ExitStack()
    wstg = [p0s.enter_context(nc.sbuf_tensor(f"wstg{i}", [128, 3072], F32))
            for i in range(2)]
    cvt = p0s.enter_context(nc.sbuf_tensor("cvt", [128, 1024], BF16))
    rrow0 = p0s.enter_context(nc.sbuf_tensor("rrow0", [1, DM], F32))
    xs = [p0s.enter_context(nc.sbuf_tensor(f"xs{i}", [128, DM], F32))
          for i in range(2)]
    ps_xp0 = p0s.enter_context(nc.psum_tensor("ps_xp0", [128, 1024], BF16))
    ps_bc0 = p0s.enter_context(nc.psum_tensor("ps_bc0", [128, 1024], F32))

    @blk.sync
    def _(sync):
        for ap_d, ap_s in [
            (ident[:], ident_c), (triu[:], triu_c),
            (onesr_bf[:], onesr_bf_c), (onesr_f[:], onesr_f_c),
            (negr_f[:], negr_f_c), (onesc_f[:], onesc_f_c),
            (logk_t[:], logk_cc), (capr[:], capr_cc), (capr1[:], capr1_cc),
            (b1l_t[:], b1l_h.rearrange("q (c p) -> (q p) c", p=128)),
            (b1h_t[:], b1h_h.rearrange("q (c p) -> (q p) c", p=128)),
            (gl_t[:], gl_h.rearrange("q (c p) -> (q p) c", p=128)),
            (gh_t[:], gh_h.rearrange("q (c p) -> (q p) c", p=128)),
            (rrow0[:], rt_h), (kvsel_t[:], kvsel_h),
            (ssel_t[:], ssel_h), (hsel_t[:], hsel_h),
        ]:
            inc("wD", sync.dma_start(out=ap_d, in_=ap_s,
                                     allow_slow_non_contiguous=True), 16)
        ev["w_small"] = V["wD"]

    @blk.vector
    def _(vector):
        w(vector, "wD", ev["w_small"])
        inc("ve", vector.memset(zero_f[:], 0.0))
        inc("ve", vector.memset(zero_bf[:], 0.0))
        inc("ve", vector.memset(eps5[:], 1e-5))
        inc("ve", vector.memset(eps30[:], 1e-30))
        ev["zeros"] = V["ve"]

    for t in range(NT):
        p = t % 2

        @blk.scalar
        def _(scalar, t=t, p=p):
            if t >= 2:
                w(scalar, "ve", ev[("swq", t - 2)])
            inc("xA", scalar.dma_start(
                out=xs[p][:], in_=x_h[128 * t:128 * (t + 1), :]), 16)

        @blk.vector
        def _(vector, t=t, p=p):
            w(vector, "xA", 16 * (t + 1))
            inc("ve", vector.tensor_reduce(out=sums1[:, t:t + 1], in_=xs[p][:],
                                           axis=AXX, op=ALU.add))
            inc("ve", vector.scalar_tensor_tensor(
                out=junk_v[:], in0=xs[p][:], scalar=1.0, in1=xs[p][:],
                op0=ALU.mult, op1=ALU.mult, accum_out=ssq1[:, t:t + 1]))
            ev[("swq", t)] = V["ve"]

    @blk.vector
    def _(vector):
        w(vector, "ve")
        inc("ve", vector.tensor_scalar(out=negm1[:], in0=sums1[:],
                                       scalar1=-1.0 / DM, scalar2=None,
                                       op0=ALU.mult))
        w(vector, "ve")
        inc("ve", vector.tensor_tensor(out=st32a[:], in0=negm1[:], in1=negm1[:],
                                       op=ALU.mult))
        inc("ve", vector.tensor_scalar(out=st32b[:], in0=ssq1[:],
                                       scalar1=1.0 / DM, scalar2=None,
                                       op0=ALU.mult))
        w(vector, "ve")
        inc("ve", vector.tensor_tensor(out=st32b[:], in0=st32b[:], in1=st32a[:],
                                       op=ALU.subtract))
        ev["var1"] = V["ve"]

    @blk.scalar
    def _(scalar):
        w(scalar, "ve", ev["var1"])
        inc("se", scalar.activation(out=st32a[:], in_=st32b[:], func=AF.Sqrt,
                                    bias=eps5[:]))
        ev["sq1"] = V["se"]

    @blk.vector
    def _(vector):
        w(vector, "se", ev["sq1"])
        inc("ve", vector.reciprocal(out=rstd1[:], in_=st32a[:]))
        w(vector, "ve")
        inc("ve", vector.tensor_tensor(out=negm1[:], in0=negm1[:], in1=rstd1[:],
                                       op=ALU.mult))
        ev["ln1"] = V["ve"]

    if dumps:
        @blk.sync
        def _(sync):
            w(sync, "ve", ev["ln1"])
            inc("wD", sync.dma_start(out=dbg["d_stats"][:, 0:32], in_=sums1[:]), 16)
            inc("wD", sync.dma_start(out=dbg["d_stats"][:, 32:64], in_=ssq1[:]), 16)
            inc("wD", sync.dma_start(out=dbg["d_stats"][:, 64:96], in_=negm1[:]), 16)
            inc("wD", sync.dma_start(out=dbg["d_stats"][:, 96:128], in_=rstd1[:]), 16)
            inc("wD", sync.dma_start(out=dbg["d_stats"][:, 128:160], in_=st32b[:]), 16)
            inc("wD", sync.dma_start(out=dbg["d_stats"][:, 160:192], in_=st32a[:]), 16)

    def load_cast(tag, src_ap, dst_bf, n_chunks, ncols, dst_stride):
        for kc in range(n_chunks):
            p = kc % 2

            @blk.sync
            def _(sync, kc=kc, p=p):
                if (tag, "c", kc - 2) in ev:
                    w(sync, "ve", ev[(tag, "c", kc - 2)])
                inc("wD", sync.dma_start(
                    out=wstg[p][:, 0:ncols],
                    in_=src_ap[128 * kc:128 * (kc + 1), :]), 16)
                ev[(tag, "l", kc)] = V["wD"]

            @blk.vector
            def _(vector, kc=kc, p=p):
                w(vector, "wD", ev[(tag, "l", kc)])
                inc("ve", vector.tensor_copy(
                    out=dst_bf[:, dst_stride * kc:dst_stride * kc + ncols],
                    in_=wstg[p][:, 0:ncols]))
                ev[(tag, "c", kc)] = V["ve"]

    load_cast("wq", Wqkv_h, Wqkv_bf, 8, 3072, 3072)
    load_cast("wg", Wg_h, Wg_bf, 8, 1024, 1024)
    load_cast("wp", Wp_h, Wp_bf, 8, 1024, 1024)

    @blk.sync
    def _(sync):
        w(sync, "ve", ev[("wp", "c", 7)])
        inc("wD", sync.dma_start(out=wstg[0][0:1, 0:DM], in_=bg_h), 16)
        inc("wD", sync.dma_start(out=wstg[0][0:1, DM:2 * DM], in_=bp_h), 16)
        inc("wD", sync.dma_start(out=wstg[0][0:1, 2 * DM:3 * DM], in_=b2l_h), 16)
        inc("wD", sync.dma_start(out=wstg[1][0:1, 0:DM], in_=b2h_h), 16)
        ev["bias_l"] = V["wD"]

    @blk.vector
    def _(vector):
        w(vector, "wD", ev["bias_l"])
        inc("ve", vector.tensor_copy(out=bg_r[:], in_=wstg[0][0:1, 0:DM]))
        inc("ve", vector.tensor_copy(out=bp_r[:], in_=wstg[0][0:1, DM:2 * DM]))
        inc("ve", vector.tensor_copy(out=b2l_r[:],
                                     in_=wstg[0][0:1, 2 * DM:3 * DM]))
        inc("ve", vector.tensor_copy(out=b2h_r[:], in_=wstg[1][0:1, 0:DM]))
        ev["bias_c"] = V["ve"]

    @blk.tensor
    def _(tensor):
        w(tensor, "wD", ev["w_small"])
        inc("te", tensor.matmul(ps_bc0[:, 0:512], onesr_f[:],
                                rrow0[0:1, 0:512], start=True, stop=True))
        inc("te", tensor.matmul(ps_bc0[:, 512:1024], onesr_f[:],
                                rrow0[0:1, 512:1024], start=True, stop=True))
        ev["rrep_mm"] = V["te"]

    @blk.vector
    def _(vector):
        w(vector, "te", ev["rrep_mm"])
        inc("ve", vector.tensor_copy(out=r_rep[:], in_=ps_bc0[:]))
        w(vector, "ve")
        inc("ve", vector.tensor_reduce(out=Sr[:], in_=r_rep[:], axis=AXX,
                                       op=ALU.add))
        ev["rrep"] = V["ve"]

    def conv_light(tag, src_ap, n_rc, n_cc, dst_write):
        for rc in range(n_rc):
            p = rc % 2

            @blk.sync
            def _(sync, rc=rc, p=p):
                if (tag, "c", rc - 2) in ev:
                    w(sync, "ve", ev[(tag, "c", rc - 2)])
                inc("wD", sync.dma_start(
                    out=wstg[p][:, 0:n_cc * 128],
                    in_=src_ap[128 * rc:128 * (rc + 1), :]), 16)
                ev[(tag, "l", rc)] = V["wD"]

            @blk.vector
            def _(vector, rc=rc, p=p):
                w(vector, "wD", ev[(tag, "l", rc)])
                if (tag, "e", rc - 1) in ev:
                    w(vector, "se", ev[(tag, "e", rc - 1)])
                inc("ve", vector.tensor_copy(out=cvt[:, 0:n_cc * 128],
                                             in_=wstg[p][:, 0:n_cc * 128]))
                ev[(tag, "c", rc)] = V["ve"]

            @blk.tensor
            def _(tensor, rc=rc):
                w(tensor, "ve", ev[(tag, "c", rc)])
                if (tag, "e", rc - 1) in ev:
                    w(tensor, "se", ev[(tag, "e", rc - 1)])
                ins = None
                for cc in range(n_cc):
                    ins = tensor.transpose(ps_xp0[:, 128 * cc:128 * (cc + 1)],
                                           cvt[:, 128 * cc:128 * (cc + 1)],
                                           ident[:])
                inc("te", ins)
                ev[(tag, "t", rc)] = V["te"]

            @blk.scalar
            def _(scalar, rc=rc):
                w(scalar, "te", ev[(tag, "t", rc)])
                dst_write(scalar, rc)
                ev[(tag, "e", rc)] = V["se"]

    def w1l_write(scalar, rc):
        ins = None
        for cc in range(8):
            ins = scalar.copy(
                out=w1lT[:, 512 * cc + 128 * rc:512 * cc + 128 * (rc + 1)],
                in_=ps_xp0[:, 128 * cc:128 * (cc + 1)])
        inc("se", ins)

    def w2l_write(scalar, rc):
        ins = None
        for cc in range(4):
            ins = scalar.copy(
                out=w2lT[:, 1024 * cc + 128 * rc:1024 * cc + 128 * (rc + 1)],
                in_=ps_xp0[:, 128 * cc:128 * (cc + 1)])
        inc("se", ins)

    conv_light("w1l", w1l_h, 4, 8, w1l_write)

    @blk.vector
    def _(vector):
        w(vector, "se", ev[("w1l", "e", 3)])
        ins = None
        for kc in range(8):
            ins = vector.tensor_scalar(
                out=w1lT[:, 512 * kc:512 * (kc + 1)],
                in0=w1lT[:, 512 * kc:512 * (kc + 1)],
                scalar1=gl_t[:, kc:kc + 1], scalar2=None, op0=ALU.mult)
        inc("ve", ins)
        ev["w1lg"] = V["ve"]

    if dumps:
        @blk.sync
        def _(sync):
            w(sync, "ve", ev["w1lg"])
            inc("wD", sync.dma_start(out=dbg["d_w1lT"], in_=w1lT[:]), 16)

    conv_light("w2l", w2l_h, 8, 4, w2l_write)
    ev["p0_done_se"] = V["se"]
    ev["p0_done_ve"] = V["ve"]

    p0s.close()

    if upto <= 0:
        return _finish([p01])

    p1s = contextlib.ExitStack()
    xT = [p1s.enter_context(nc.sbuf_tensor(f"x_{i}", [128, DM], F32))
          for i in range(2)]
    xn1 = [p1s.enter_context(nc.sbuf_tensor(f"xn1_{i}", [128, DM], BF16))
           for i in range(2)]
    xn1T = [p1s.enter_context(nc.sbuf_tensor(f"xn1T_{i}", [128, DM], BF16))
            for i in range(2)]
    gbuf = [p1s.enter_context(nc.sbuf_tensor(f"g_{i}", [128, DM], F32))
            for i in range(2)]
    qgb = [p1s.enter_context(nc.sbuf_tensor(f"qg_{i}", [128, DM], F32))
           for i in range(2)]
    ebuf = [p1s.enter_context(nc.sbuf_tensor(f"e_{i}", [128, DM], F32))
            for i in range(2)]
    qp = [p1s.enter_context(nc.sbuf_tensor(f"qp_{i}", [128, DM], BF16))
          for i in range(2)]
    kp = [p1s.enter_context(nc.sbuf_tensor(f"kp_{i}", [128, DM], BF16))
          for i in range(2)]
    vsb = [p1s.enter_context(nc.sbuf_tensor(f"v_{i}", [128, 1040], BF16))
           for i in range(2)]
    ps_xp = p1s.enter_context(nc.psum_tensor("ps_xp", [128, 1024], BF16))
    ps_a = p1s.enter_context(nc.psum_tensor("ps_a", [128, 1024], F32))
    ps_b = p1s.enter_context(nc.psum_tensor("ps_b", [128, 1024], F32))
    ps_kv = p1s.enter_context(nc.psum_tensor("ps_kv", [64, 1024], F32))
    ps_ks = p1s.enter_context(nc.psum_tensor("ps_ks", [64, 16], F32))

    @blk.vector
    def _(vector):
        w(vector, "ve", ev["zeros"])
        for i in range(2):
            vv = vsb[i][:].rearrange("p (h j) -> p h j", h=16)
            inc("ve", vector.memset(vv[:, :, 64:65], 1.0))
        ev["vones"] = V["ve"]

    @blk.scalar
    def _(scalar):
        w(scalar, "ve", ev["ln1"])
        for t in range(2):
            inc("xA", scalar.dma_start(
                out=xT[t][:], in_=x_h[128 * t:128 * (t + 1), :]), 16)

    def proj_qkv(tensor, p, which, psd):
        ins = None
        for kc in range(8):
            tensor.ldweights(xn1T[p][:, 128 * kc:128 * (kc + 1)])
            for n in range(2):
                ins = tensor.matmul(
                    psd[:, 512 * n:512 * (n + 1)],
                    xn1T[p][:, 128 * kc:128 * (kc + 1)],
                    Wqkv_bf[:, 3072 * kc + DM * which + 512 * n:
                            3072 * kc + DM * which + 512 * (n + 1)],
                    start=(kc == 0), stop=(kc == 7))
        return ins

    def dbg_dump(name, ap, sem, val):
        if dumps:
            @blk.sync
            def _(sync):
                w(sync, sem, val)
                inc("wD", sync.dma_start(out=dbg[name], in_=ap), 16)

    for t in range(NT):
        p = t % 2

        @blk.vector
        def _(vector, t=t, p=p):
            w(vector, "xA", 16 * (NT + t + 1))
            w(vector, "ve", ev["vones"])
            inc("ve", vector.tensor_scalar(
                out=xn1[p][:], in0=xT[p][:], scalar1=rstd1[:, t:t + 1],
                scalar2=negm1[:, t:t + 1], op0=ALU.mult, op1=ALU.add))
            ev[("xn1", t)] = V["ve"]

        @blk.tensor
        def _(tensor, t=t, p=p):
            w(tensor, "ve", ev[("xn1", t)])
            if t > 0:
                w(tensor, "se", ev[("xpE", t - 1)])
            else:
                w(tensor, "se", ev["p0_done_se"])
            ins = None
            for kc in range(8):
                ins = tensor.transpose(ps_xp[:, 128 * kc:128 * (kc + 1)],
                                       xn1[p][:, 128 * kc:128 * (kc + 1)],
                                       ident[:])
            inc("te", ins)
            ev[("xp", t)] = V["te"]

        @blk.scalar
        def _(scalar, t=t, p=p):
            w(scalar, "te", ev[("xp", t)])
            inc("se", scalar.copy(out=xn1T[p][:], in_=ps_xp[:]))
            ev[("xpE", t)] = V["se"]
            if t == 0 and dumps:
                inc("wD", scalar.dma_start(out=dbg["d_xn1T_t0"],
                                           in_=xn1T[0][:]), 16)
            if t + 2 < NT:
                w(scalar, "ve", ev[("xn1", t)])
                inc("xA", scalar.dma_start(
                    out=xT[p][:], in_=x_h[128 * (t + 2):128 * (t + 3), :]), 16)

        @blk.tensor
        def _(tensor, t=t, p=p):
            w(tensor, "se", ev[("xpE", t)])
            w(tensor, "ve", ev["p0_done_ve"] if t == 0 else ev[("vE", t - 1)])
            ins = None
            for kc in range(8):
                tensor.ldweights(xn1T[p][:, 128 * kc:128 * (kc + 1)])
                for n in range(2):
                    ins = tensor.matmul(
                        ps_a[:, 512 * n:512 * (n + 1)],
                        xn1T[p][:, 128 * kc:128 * (kc + 1)],
                        Wg_bf[:, DM * kc + 512 * n:DM * kc + 512 * (n + 1)],
                        start=(kc == 0), stop=False)
            tensor.ldweights(onesr_bf[:])
            for n in range(2):
                ins = tensor.matmul(ps_a[:, 512 * n:512 * (n + 1)], onesr_bf[:],
                                    bg_r[0:1, 512 * n:512 * (n + 1)],
                                    start=False, stop=True)
            inc("te", ins)
            ev[("gmm", t)] = V["te"]

        @blk.scalar
        def _(scalar, t=t, p=p):
            w(scalar, "te", ev[("gmm", t)])
            if t > 0:
                w(scalar, "ve", ev[("gfin", t - 1)])
            inc("se", scalar.activation(out=gbuf[p][:], in_=ps_a[:],
                                        func=AF.Tanh, scale=0.5))
            ev[("gt", t)] = V["se"]

        @blk.vector
        def _(vector, t=t, p=p):
            w(vector, "se", ev[("gt", t)])
            inc("ve", vector.tensor_scalar(out=gbuf[p][:], in0=gbuf[p][:],
                                           scalar1=0.5, scalar2=0.5,
                                           op0=ALU.mult, op1=ALU.add))
            ev[("gfin", t)] = V["ve"]

        if t == 0:
            dbg_dump("d_g_t0", gbuf[0][:], "ve", ev[("gfin", 0)])

        @blk.tensor
        def _(tensor, t=t, p=p):
            w(tensor, "ve", ev["p0_done_ve"] if t == 0 else ev[("kg", t - 1)])
            inc("te", proj_qkv(tensor, p, 0, ps_b))
            ev[("qmm", t)] = V["te"]

        @blk.vector
        def _(vector, t=t, p=p):
            w(vector, "te", ev[("qmm", t)])
            w(vector, "ve", ev[("gfin", t)])
            if t > 1:
                w(vector, "qW", 16 * (t - 1))
            inc("ve", vector.tensor_tensor(out=qgb[p][:], in0=ps_b[:],
                                           in1=gbuf[p][:], op=ALU.mult))
            ev[("qg", t)] = V["ve"]

        @blk.scalar
        def _(scalar, t=t, p=p):
            w(scalar, "ve", ev[("qg", t)])
            inc("se", scalar.activation(out=ebuf[p][:], in_=qgb[p][:],
                                        func=AF.Relu, scale=-1.0))
            w(scalar, "se")
            inc("se", scalar.activation(out=ebuf[p][:], in_=ebuf[p][:],
                                        func=AF.Exp, scale=-1.0))
            ev[("qe", t)] = V["se"]

        @blk.vector
        def _(vector, t=t, p=p):
            w(vector, "se", ev[("qe", t)])
            inc("ve", vector.scalar_tensor_tensor(
                out=qp[p][:], in0=qgb[p][:], scalar=0.0, in1=ebuf[p][:],
                op0=ALU.max, op1=ALU.add))
            ev[("qprime", t)] = V["ve"]

        @blk.scalar
        def _(scalar, t=t, p=p):
            w(scalar, "ve", ev[("qprime", t)])
            inc("qW", scalar.dma_start(out=qprime_d[128 * t:128 * (t + 1), :],
                                       in_=qp[p][:]), 16)

        @blk.tensor
        def _(tensor, t=t, p=p):
            w(tensor, "ve", ev[("qg", t)])
            inc("te", proj_qkv(tensor, p, 1, ps_b))
            ev[("kmm", t)] = V["te"]

        @blk.vector
        def _(vector, t=t, p=p):
            w(vector, "te", ev[("kmm", t)])
            inc("ve", vector.tensor_tensor(out=qgb[p][:], in0=ps_b[:],
                                           in1=gbuf[p][:], op=ALU.mult))
            ev[("kg", t)] = V["ve"]

        @blk.scalar
        def _(scalar, t=t, p=p):
            w(scalar, "ve", ev[("kg", t)])
            inc("se", scalar.activation(out=ebuf[p][:], in_=qgb[p][:],
                                        func=AF.Relu, scale=-1.0))
            w(scalar, "se")
            inc("se", scalar.activation(out=ebuf[p][:], in_=ebuf[p][:],
                                        func=AF.Exp, scale=-1.0))
            ev[("ke", t)] = V["se"]

        @blk.vector
        def _(vector, t=t, p=p):
            w(vector, "se", ev[("ke", t)])
            if t > 1:
                w(vector, "te", ev[("kv", t - 2)])
            inc("ve", vector.scalar_tensor_tensor(
                out=kp[p][:], in0=qgb[p][:], scalar=0.0, in1=ebuf[p][:],
                op0=ALU.max, op1=ALU.add))
            ev[("kprime", t)] = V["ve"]

        @blk.tensor
        def _(tensor, t=t, p=p):
            w(tensor, "se", ev[("gt", t)])
            inc("te", proj_qkv(tensor, p, 2, ps_a))
            ev[("vmm", t)] = V["te"]

        @blk.vector
        def _(vector, t=t, p=p):
            w(vector, "te", ev[("vmm", t)])
            if t > 1:
                w(vector, "te", ev[("kv", t - 2)])
            vv = vsb[p][:].rearrange("p (h j) -> p h j", h=16)
            pa = ps_a[:].rearrange("p (h j) -> p h j", h=16)
            inc("ve", vector.tensor_copy(out=vv[:, :, 0:64], in_=pa))
            ev[("vE", t)] = V["ve"]

        @blk.tensor
        def _(tensor, t=t, p=p):
            w(tensor, "ve", ev[("kprime", t)])
            w(tensor, "ve", ev[("vE", t)])
            ins = None
            for hh in range(NH):
                tensor.ldweights(kp[p][:, 64 * hh:64 * (hh + 1)])
                tensor.matmul(ps_kv[:, 64 * hh:64 * (hh + 1)],
                              kp[p][:, 64 * hh:64 * (hh + 1)],
                              vsb[p][:, 65 * hh:65 * hh + 64],
                              start=(t == 0 and hh in (0, 8)),
                              stop=(t == NT - 1))
                ins = tensor.matmul(ps_ks[:, hh:hh + 1],
                                    kp[p][:, 64 * hh:64 * (hh + 1)],
                                    vsb[p][:, 65 * hh + 64:65 * hh + 65],
                                    start=(t == 0 and hh == 0),
                                    stop=(t == NT - 1))
            inc("te", ins)
            ev[("kv", t)] = V["te"]

    if upto <= 1:
        return _finish([p1s, p01])

    @blk.vector
    def _(vector):
        w(vector, "te", ev[("kv", NT - 1)])
        inc("ve", vector.tensor_copy(out=kvs[0:64, 0:1024], in_=ps_kv[:]))
        inc("ve", vector.tensor_copy(out=kvs[0:64, 1024:1040], in_=ps_ks[:]))
        ev["kvE"] = V["ve"]

    @blk.sync
    def _(sync):
        w(sync, "ve", ev["kvE"])
        inc("kvD", sync.dma_start(
            out=kvag_in[0:1, :].rearrange("q (p j) -> (q p) j", p=64),
            in_=kvs[:]), 16)

    @blk.gpsimd
    def _(gpsimd):
        w(gpsimd, "kvD", 16)
        gpsimd.collective_compute(
            "AllGather", ALU.bypass, replica_groups=RG8,
            ins=[kvag_in[:]], outs=[kvag_out[:]],
        ).then_inc(S["cc1"], 1)
        gpsimd.wait_ge(S["cc1"], 1)
    V["cc1"] = 1

    p1s.close()

    agc = contextlib.ExitStack()
    kvrows = agc.enter_context(nc.sbuf_tensor("kvrows", [64, 8 * 1040], F32))
    ps_ag = agc.enter_context(nc.psum_tensor("ps_ag", [128, 1024], F32))

    @blk.sync
    def _(sync):
        w(sync, "cc1", 1)
        w(sync, "qW", 16 * NT)
        for r in range(8):
            inc("kvD", sync.dma_start(
                out=kvrows[0:64, 1040 * r:1040 * (r + 1)],
                in_=kvag_out[r:r + 1, :].rearrange("q (p j) -> (q p) j", p=64)),
                16)
        ev["kvrl"] = V["kvD"]

    @blk.vector
    def _(vector):
        w(vector, "kvD", ev["kvrl"])
        inc("ve", vector.memset(kv_all[:], 0.0))
        for r in range(8):
            w(vector, "ve")
            inc("ve", vector.scalar_tensor_tensor(
                out=kv_all[:], in0=kvrows[0:64, 1040 * r:1040 * (r + 1)],
                scalar=kvsel_t[:, r:r + 1], in1=kv_all[:],
                op0=ALU.mult, op1=ALU.add))
        ev["kvall"] = V["ve"]

    @blk.sync
    def _(sync):
        w(sync, "ve", ev["kvall"])
        for hh in range(NH):
            inc("sD", sync.dma_start(
                out=kvs[0:1, 64 * hh:64 * (hh + 1)],
                in_=kv_all[0:64, 1024 + hh:1024 + hh + 1]), 16)
        ev["ksumrow"] = V["sD"]

    @blk.tensor
    def _(tensor):
        w(tensor, "sD", ev["ksumrow"])
        w(tensor, "ve", ev["kvall"])
        inc("te", tensor.matmul(ps_ag[:, 0:512], onesr_f[:], kvs[0:1, 0:512],
                                start=True, stop=True))
        inc("te", tensor.matmul(ps_ag[:, 512:1024], onesr_f[:],
                                kvs[0:1, 512:1024], start=True, stop=True))
        ev["ksmm"] = V["te"]

    @blk.vector
    def _(vector):
        w(vector, "te", ev["ksmm"])
        inc("ve", vector.tensor_copy(out=ksum_rep[:], in_=ps_ag[:]))
        inc("ve", vector.memset(kv_bd[:], 0.0))
        w(vector, "ve")
        ins = None
        for c in range(8):
            vector.tensor_copy(out=kv_bd[0:64, 128 * c:128 * c + 64],
                               in_=kv_all[0:64, 128 * c:128 * c + 64])
            ins = vector.tensor_copy(
                out=kv_bd[64:128, 128 * c + 64:128 * (c + 1)],
                in_=kv_all[0:64, 128 * c + 64:128 * (c + 1)])
        inc("ve", ins)
        ev["kvbd"] = V["ve"]

    if dumps:
        @blk.sync
        def _(sync):
            w(sync, "ve", ev["kvbd"])
            inc("wD", sync.dma_start(out=dbg["d_kv"], in_=kv_all[:]), 16)
            w(sync, "qW", 16 * NT)
            inc("wD", sync.dma_start(out=dbg["d_qp_t0"],
                                     in_=qprime_d[0:128, :]), 16)

    agc.close()
    p01.close()

    if upto <= 2:
        return _finish([])

    xnsc = contextlib.ExitStack()
    xn2f = xnsc.enter_context(nc.sbuf_tensor("xn2f", [128, NT * DM], BF16))

    hvy = contextlib.ExitStack()
    hstg = [hvy.enter_context(nc.sbuf_tensor(f"hstg{i}", [128, 2048], F32))
            for i in range(2)]
    hcvt = hvy.enter_context(nc.sbuf_tensor("hcvt", [128, 1024], BF16))
    ps_xph = hvy.enter_context(nc.psum_tensor("ps_xph", [128, 1024], BF16))

    def conv_heavy(tag, src_ap, dst_sub, n_rc, n_cc, gamma_t):
        ngq = n_cc // 8
        prev_store = [None]
        for rc in range(n_rc):
            p = rc % 2

            @blk.sync
            def _(sync, rc=rc, p=p):
                if (tag, "c", rc - 2) in ev:
                    w(sync, "ve", ev[(tag, "c", rc - 2)])
                else:
                    w(sync, "ve", ev["kvbd"])
                    w(sync, "qW", 16 * NT)
                inc("hD", sync.dma_start(
                    out=hstg[p][:, 0:n_cc * 128],
                    in_=src_ap[128 * rc:128 * (rc + 1), :]), 16)
                ev[(tag, "l", rc)] = V["hD"]

            for gq in range(ngq):

                @blk.vector
                def _(vector, rc=rc, p=p, gq=gq):
                    w(vector, "hD", ev[(tag, "l", rc)])
                    if prev_store[0] is not None:
                        w(vector, "hS", prev_store[0])
                    inc("ve", vector.tensor_copy(
                        out=hcvt[:, 0:1024],
                        in_=hstg[p][:, 1024 * gq:1024 * (gq + 1)]))
                    ev[(tag, "cv", rc, gq)] = V["ve"]
                    if gq == ngq - 1:
                        ev[(tag, "c", rc)] = V["ve"]

                @blk.tensor
                def _(tensor, rc=rc, gq=gq):
                    w(tensor, "ve", ev[(tag, "cv", rc, gq)])
                    ins = None
                    for cc in range(8):
                        ins = tensor.transpose(
                            ps_xph[:, 128 * cc:128 * (cc + 1)],
                            hcvt[:, 128 * cc:128 * (cc + 1)], ident[:])
                    inc("te", ins)
                    ev[(tag, "t", rc, gq)] = V["te"]

                @blk.vector
                def _(vector, rc=rc, gq=gq):
                    w(vector, "te", ev[(tag, "t", rc, gq)])
                    if gamma_t is not None:
                        gg = gamma_t[:, 8 * gq:8 * (gq + 1)]
                        bc = gg.unsqueeze(2).broadcast_to([128, 8, 128])
                        inc("ve", vector.tensor_tensor(
                            out=hcvt[:].rearrange("p (c j) -> p c j", c=8),
                            in0=ps_xph[:].rearrange("p (c j) -> p c j", c=8),
                            in1=bc, op=ALU.mult))
                    else:
                        inc("ve", vector.tensor_copy(out=hcvt[:], in_=ps_xph[:]))
                    ev[(tag, "ev", rc, gq)] = V["ve"]

                @blk.sync
                def _(sync, rc=rc, gq=gq):
                    w(sync, "ve", ev[(tag, "ev", rc, gq)])
                    dst = dst_sub.rearrange("(c p) r2 -> p c r2", p=128)
                    inc("hS", sync.dma_start(
                        out=dst[:, 8 * gq:8 * (gq + 1),
                                128 * rc:128 * (rc + 1)],
                        in_=hcvt[:].rearrange("p (c j) -> p c j", c=8)), 16)
                    prev_store[0] = V["hS"]
                    ev[(tag, "s", rc, gq)] = V["hS"]

    conv_heavy("w1h", w1h_h, w1hT_d[:, :], 32, 8, gh_t)
    conv_heavy("w2a", w2h_h[:, 0:2048], w2hT_d[0:2048, :], 8, 16, None)
    conv_heavy("w2b", w2h_h[:, 2048:4096], w2hT_d[2048:4096, :], 8, 16, None)
    ev["hconv"] = V["hS"]

    if upto <= 3:
        return _finish([hvy, xnsc])

    _barrier()

    p2s = contextlib.ExitStack()
    qpb = [p2s.enter_context(nc.sbuf_tensor(f"qpb_{i}", [128, DM], BF16))
           for i in range(2)]
    xT2 = [p2s.enter_context(nc.sbuf_tensor(f"xT2_{i}", [128, DM], F32))
           for i in range(2)]
    npf = p2s.enter_context(nc.sbuf_tensor("npf", [128, DM], F32))
    n16 = p2s.enter_context(nc.sbuf_tensor("n16", [128, 16], F32))
    q2b = [p2s.enter_context(nc.sbuf_tensor(f"q2b_{i}", [128, DM], BF16))
           for i in range(2)]
    q2T = [p2s.enter_context(nc.sbuf_tensor(f"q2T_{i}", [128, DM], BF16))
           for i in range(2)]
    OT = [p2s.enter_context(nc.sbuf_tensor(f"OT_{i}", [128, DM], BF16))
          for i in range(2)]
    tat = [p2s.enter_context(nc.sbuf_tensor(f"tat_{i}", [128, DM], F32))
           for i in range(2)]
    x2b = [p2s.enter_context(nc.sbuf_tensor(f"x2b_{i}", [128, DM], F32))
           for i in range(2)]
    ps_xp2 = p2s.enter_context(nc.psum_tensor("ps_xp2", [128, 1024], BF16))
    ps_o = p2s.enter_context(nc.psum_tensor("ps_o", [128, 1024], F32))
    ps_at = p2s.enter_context(nc.psum_tensor("ps_at", [128, 1024], F32))

    @blk.scalar
    def _(scalar):
        w(scalar, "ve", ev["kvbd"])
        w(scalar, "qW", 16 * NT)
        for t in range(2):
            inc("qR", scalar.dma_start(
                out=qpb[t][:], in_=qprime_d[128 * t:128 * (t + 1), :]), 16)
            inc("xB", scalar.dma_start(
                out=xT2[t][:], in_=x_h[128 * t:128 * (t + 1), :]), 16)

    for t in range(NT):
        p = t % 2

        @blk.vector
        def _(vector, t=t, p=p):
            w(vector, "qR", 16 * (t + 1))
            w(vector, "ve", ev["kvbd"])
            if t > 1:
                w(vector, "te", ev[("q2T", t - 2)])
            inc("ve", vector.tensor_tensor(out=npf[:], in0=qpb[p][:],
                                           in1=ksum_rep[:], op=ALU.mult))
            w(vector, "ve")
            inc("ve", vector.tensor_reduce(
                out=n16[:].unsqueeze(2),
                in_=npf[:].rearrange("p (h j) -> p h j", h=16),
                axis=AXX, op=ALU.add))
            w(vector, "ve")
            inc("ve", vector.tensor_scalar(out=n16[:], in0=n16[:],
                                           scalar1=1e-8, scalar2=None,
                                           op0=ALU.add))
            w(vector, "ve")
            inc("ve", vector.reciprocal(out=n16[:], in_=n16[:]))
            w(vector, "ve")
            bc = n16[:].unsqueeze(2).broadcast_to([128, 16, 64])
            inc("ve", vector.tensor_tensor(
                out=q2b[p][:].rearrange("p (h j) -> p h j", h=16),
                in0=qpb[p][:].rearrange("p (h j) -> p h j", h=16),
                in1=bc, op=ALU.mult))
            ev[("q2", t)] = V["ve"]

        @blk.tensor
        def _(tensor, t=t, p=p):
            w(tensor, "ve", ev[("q2", t)])
            if t > 0:
                w(tensor, "se", ev[("q2TE", t - 1)])
            ins = None
            for kc in range(8):
                ins = tensor.transpose(ps_xp2[:, 128 * kc:128 * (kc + 1)],
                                       q2b[p][:, 128 * kc:128 * (kc + 1)],
                                       ident[:])
            inc("te", ins)
            ev[("q2T", t)] = V["te"]

        @blk.scalar
        def _(scalar, t=t, p=p):
            w(scalar, "te", ev[("q2T", t)])
            inc("se", scalar.copy(out=q2T[p][:], in_=ps_xp2[:]))
            ev[("q2TE", t)] = V["se"]

        @blk.tensor
        def _(tensor, t=t, p=p):
            w(tensor, "se", ev[("q2TE", t)])
            if t > 0:
                w(tensor, "se", ev[("otE", t - 1)])
            ins = None
            for c in range(8):
                tensor.ldweights(kv_bd[:, 128 * c:128 * (c + 1)])
                ins = tensor.matmul(ps_o[:, 128 * c:128 * (c + 1)],
                                    kv_bd[:, 128 * c:128 * (c + 1)],
                                    q2T[p][:, 128 * c:128 * (c + 1)],
                                    start=True, stop=True)
            inc("te", ins)
            ev[("omm", t)] = V["te"]

        @blk.scalar
        def _(scalar, t=t, p=p):
            w(scalar, "te", ev[("omm", t)])
            inc("se", scalar.copy(out=OT[p][:], in_=ps_o[:]))
            ev[("otE", t)] = V["se"]

        @blk.tensor
        def _(tensor, t=t, p=p):
            w(tensor, "se", ev[("otE", t)])
            if t > 0:
                w(tensor, "se", ev[("tnorm", t - 1)])
                w(tensor, "ve", ev[("sat", t - 1)])
            ins = None
            for kc in range(8):
                tensor.ldweights(OT[p][:, 128 * kc:128 * (kc + 1)])
                for n in range(2):
                    ins = tensor.matmul(
                        ps_at[:, 512 * n:512 * (n + 1)],
                        OT[p][:, 128 * kc:128 * (kc + 1)],
                        Wp_bf[:, DM * kc + 512 * n:DM * kc + 512 * (n + 1)],
                        start=(kc == 0), stop=False)
            tensor.ldweights(onesr_bf[:])
            for n in range(2):
                ins = tensor.matmul(ps_at[:, 512 * n:512 * (n + 1)],
                                    onesr_bf[:],
                                    bp_r[0:1, 512 * n:512 * (n + 1)],
                                    start=False, stop=True)
            inc("te", ins)
            ev[("atmm", t)] = V["te"]

        @blk.vector
        def _(vector, t=t, p=p):
            w(vector, "te", ev[("atmm", t)])
            inc("ve", vector.tensor_reduce(out=t1c[:], in_=ps_at[:], axis=AXX,
                                           op=ALU.add))
            ev[("sat", t)] = V["ve"]

        @blk.scalar
        def _(scalar, t=t, p=p):
            w(scalar, "ve", ev[("sat", t)])
            inc("se", scalar.activation(out=junk_s[:], in_=ps_at[:],
                                        func=AF.Square, accum_out=t2c[:]))
            ev[("sat2", t)] = V["se"]

        @blk.vector
        def _(vector, t=t, p=p):
            w(vector, "se", ev[("sat2", t)])
            w(vector, "ve")
            inc("ve", vector.tensor_scalar(out=t1c[:], in0=t1c[:],
                                           scalar1=1.0 / DM, scalar2=None,
                                           op0=ALU.mult))
            w(vector, "ve")
            inc("ve", vector.tensor_tensor(out=t3c[:], in0=t1c[:], in1=t1c[:],
                                           op=ALU.mult))
            inc("ve", vector.tensor_scalar(out=t2c[:], in0=t2c[:],
                                           scalar1=1.0 / DM, scalar2=None,
                                           op0=ALU.mult))
            w(vector, "ve")
            inc("ve", vector.tensor_tensor(out=t2c[:], in0=t2c[:], in1=t3c[:],
                                           op=ALU.subtract))
            ev[("vat", t)] = V["ve"]

        @blk.scalar
        def _(scalar, t=t, p=p):
            w(scalar, "ve", ev[("vat", t)])
            inc("se", scalar.activation(out=t3c[:], in_=t2c[:], func=AF.Sqrt,
                                        bias=eps5[:]))
            ev[("sqat", t)] = V["se"]

        @blk.vector
        def _(vector, t=t, p=p):
            w(vector, "se", ev[("sqat", t)])
            inc("ve", vector.reciprocal(out=t3c[:], in_=t3c[:]))
            w(vector, "ve")
            inc("ve", vector.scalar_tensor_tensor(
                out=t4c[:], in0=t1c[:], scalar=-1.0, in1=t3c[:],
                op0=ALU.mult, op1=ALU.mult))
            ev[("rsat", t)] = V["ve"]

        @blk.scalar
        def _(scalar, t=t, p=p):
            w(scalar, "ve", ev[("rsat", t)])
            inc("se", scalar.activation(out=tat[p][:], in_=ps_at[:],
                                        func=AF.Identity, scale=t3c[:],
                                        bias=t4c[:]))
            ev[("tnorm", t)] = V["se"]

        @blk.vector
        def _(vector, t=t, p=p):
            w(vector, "se", ev[("tnorm", t)])
            w(vector, "xB", 16 * (t + 1))
            inc("ve", vector.tensor_tensor(out=x2b[p][:], in0=xT2[p][:],
                                           in1=tat[p][:], op=ALU.add))
            ev[("x2", t)] = V["ve"]

        @blk.scalar
        def _(scalar, t=t, p=p):
            if t + 2 < NT:
                w(scalar, "ve", ev[("q2", t)])
                inc("qR", scalar.dma_start(
                    out=qpb[p][:],
                    in_=qprime_d[128 * (t + 2):128 * (t + 3), :]), 16)
                w(scalar, "ve", ev[("x2", t)])
                inc("xB", scalar.dma_start(
                    out=xT2[p][:], in_=x_h[128 * (t + 2):128 * (t + 3), :]), 16)

        @blk.gpsimd
        def _(gpsimd, t=t, p=p):
            w(gpsimd, "ve", ev[("x2", t)])
            inc("x2W", gpsimd.dma_start(out=x2_d[128 * t:128 * (t + 1), :],
                                        in_=x2b[p][:]), 16)

        @blk.scalar
        def _(scalar, t=t, p=p):
            w(scalar, "ve", ev[("x2", t)])
            inc("se", scalar.activation(out=junk_s[:], in_=x2b[p][:],
                                        func=AF.Square, accum_out=t2c[:]))
            ev[("sx22", t)] = V["se"]

        @blk.vector
        def _(vector, t=t, p=p):
            w(vector, "se", ev[("sx22", t)])
            inc("ve", vector.tensor_scalar(out=t1c[:], in0=sums1[:, t:t + 1],
                                           scalar1=1.0 / DM, scalar2=None,
                                           op0=ALU.mult))
            w(vector, "ve")
            inc("ve", vector.tensor_tensor(out=t3c[:], in0=t1c[:], in1=t1c[:],
                                           op=ALU.mult))
            inc("ve", vector.tensor_scalar(out=t2c[:], in0=t2c[:],
                                           scalar1=1.0 / DM, scalar2=None,
                                           op0=ALU.mult))
            w(vector, "ve")
            inc("ve", vector.tensor_tensor(out=t2c[:], in0=t2c[:], in1=t3c[:],
                                           op=ALU.subtract))
            ev[("v2", t)] = V["ve"]

        @blk.scalar
        def _(scalar, t=t, p=p):
            w(scalar, "ve", ev[("v2", t)])
            scalar.activation(out=t3c[:], in_=t2c[:], func=AF.Sqrt, bias=eps30[:])
            inc("se", scalar.activation(out=t5c[:], in_=t2c[:], func=AF.Sqrt,
                                        bias=eps5[:]))
            ev[("sq2", t)] = V["se"]

        @blk.vector
        def _(vector, t=t, p=p):
            w(vector, "se", ev[("sq2", t)])
            inc("ve", vector.reciprocal(out=t3c[:], in_=t3c[:]))
            inc("ve", vector.reciprocal(out=t5c[:], in_=t5c[:]))
            w(vector, "ve")
            inc("ve", vector.scalar_tensor_tensor(
                out=t4c[:], in0=t1c[:], scalar=-1.0, in1=t3c[:],
                op0=ALU.mult, op1=ALU.mult))
            ev[("r2", t)] = V["ve"]

        @blk.scalar
        def _(scalar, t=t, p=p):
            w(scalar, "ve", ev[("r2", t)])
            inc("se", scalar.activation(out=xn2f[:, DM * t:DM * (t + 1)],
                                        in_=x2b[p][:], func=AF.Identity,
                                        scale=t3c[:], bias=t4c[:]))
            ev[("xn2f", t)] = V["se"]

        @blk.vector
        def _(vector, t=t, p=p):
            w(vector, "ve", ev[("r2", t)])
            inc("ve", vector.scalar_tensor_tensor(
                out=junk_v[:], in0=x2b[p][:], scalar=1.0, in1=r_rep[:],
                op0=ALU.mult, op1=ALU.mult, accum_out=t2c[:]))
            inc("ve", vector.tensor_tensor(out=t1c[:], in0=t1c[:], in1=Sr[:],
                                           op=ALU.mult))
            w(vector, "ve")
            inc("ve", vector.tensor_tensor(out=t2c[:], in0=t2c[:], in1=t1c[:],
                                           op=ALU.subtract))
            w(vector, "ve")
            inc("ve", vector.tensor_tensor(out=s_mine[:, t:t + 1], in0=t2c[:],
                                           in1=t5c[:], op=ALU.mult))
            ev[("scol", t)] = V["ve"]

    if dumps:
        @blk.sync
        def _(sync):
            w(sync, "ve", ev[("x2", 0)])
            inc("wD", sync.dma_start(out=dbg["d_x2_t0"], in_=x2b[0][:]), 16)

    p2s.close()
    hvy.close()

    if upto <= 4:
        return _finish([xnsc])

    _barrier()

    li = contextlib.ExitStack()
    xnT = [li.enter_context(nc.sbuf_tensor(f"xnT_{i}", [128, 8 * 512], BF16))
           for i in range(2)]
    h1lT = li.enter_context(nc.sbuf_tensor("h1lT", [128, 4 * 512], BF16))
    lob = [li.enter_context(nc.sbuf_tensor(f"lob_{i}", [128, DM], F32))
           for i in range(2)]
    ps_xpl = li.enter_context(nc.psum_tensor("ps_xpl", [128, 1024], BF16))
    ps_h1 = [li.enter_context(nc.psum_tensor(f"ps_h1_{i}", [128, 512], F32))
             for i in range(2)]
    ps_lo = li.enter_context(nc.psum_tensor("ps_lo", [128, 1024], F32))

    for st in range(8):
        pq = st % 2

        for q in range(4):
            @blk.tensor
            def _(tensor, st=st, q=q, pq=pq):
                t = 4 * st + q
                w(tensor, "ve")
                w(tensor, "se", ev[("xn2f", t)])
                if ("lxE", st, q - 1) in ev:
                    w(tensor, "se", ev[("lxE", st, q - 1)])
                elif ("lxE", st - 1, 3) in ev:
                    w(tensor, "se", ev[("lxE", st - 1, 3)])
                if st == 0 and q == 0:
                    w(tensor, "se", ev[("q2TE", NT - 1)])
                    w(tensor, "ve", ev[("scol", NT - 1)])
                ins = None
                for kc in range(8):
                    ins = tensor.transpose(
                        ps_xpl[:, 128 * kc:128 * (kc + 1)],
                        xn2f[:, DM * t + 128 * kc:DM * t + 128 * (kc + 1)],
                        ident[:])
                inc("te", ins)
                ev[("lx", st, q)] = V["te"]

            @blk.scalar
            def _(scalar, st=st, q=q, pq=pq):
                w(scalar, "ve")
                w(scalar, "te", ev[("lx", st, q)])
                dst = xnT[pq][:].rearrange("p (kc j) -> p kc j", kc=8)
                inc("se", scalar.copy(
                    out=dst[:, :, 128 * q:128 * (q + 1)],
                    in_=ps_xpl[:].rearrange("p (kc j) -> p kc j", kc=8)))
                ev[("lxE", st, q)] = V["se"]

        if li_mode < 2:
            continue
        for hc in range(4):
            @blk.tensor
            def _(tensor, st=st, hc=hc, pq=pq):
                w(tensor, "ve")
                w(tensor, "se", ev[("lxE", st, 3)])
                if ("lh1e", st, hc - 2) in ev:
                    w(tensor, "se", ev[("lh1e", st, hc - 2)])
                elif st > 0:
                    w(tensor, "se", ev[("lh1e", st - 1, 2 + hc)])
                ins = None
                for kc in range(8):
                    tensor.ldweights(
                        w1lT[:, 512 * kc + 128 * hc:512 * kc + 128 * (hc + 1)])
                    ins = tensor.matmul(
                        ps_h1[hc % 2][:],
                        w1lT[:, 512 * kc + 128 * hc:512 * kc + 128 * (hc + 1)],
                        xnT[pq][:, 512 * kc:512 * (kc + 1)],
                        start=(kc == 0), stop=(kc == 7))
                inc("te", ins)
                ev[("lh1", st, hc)] = V["te"]

            @blk.scalar
            def _(scalar, st=st, hc=hc):
                w(scalar, "ve")
                w(scalar, "te", ev[("lh1", st, hc)])
                inc("se", scalar.activation(
                    out=h1lT[:, 512 * hc:512 * (hc + 1)], in_=ps_h1[hc % 2][:],
                    func=AF.Gelu, bias=b1l_t[:, hc:hc + 1]))
                ev[("lh1e", st, hc)] = V["se"]

        if li_mode < 3:
            continue
        if dumps and st == 0:
            @blk.sync
            def _(sync):
                w(sync, "se", ev[("lh1e", 0, 3)])
                inc("wD", sync.dma_start(out=dbg["d_h1l_s0"], in_=h1lT[:]), 16)

        for q in range(4):
            @blk.tensor
            def _(tensor, st=st, q=q):
                w(tensor, "ve")
                w(tensor, "se", ev[("lh1e", st, 3)])
                if ("loE", st, q - 1) in ev:
                    w(tensor, "ve", ev[("loE", st, q - 1)])
                elif st > 0:
                    w(tensor, "ve", ev[("loE", st - 1, 3)])
                ins = None
                for hc in range(4):
                    tensor.ldweights(
                        h1lT[:, 512 * hc + 128 * q:512 * hc + 128 * (q + 1)])
                    for n in range(2):
                        ins = tensor.matmul(
                            ps_lo[:, 512 * n:512 * (n + 1)],
                            h1lT[:, 512 * hc + 128 * q:512 * hc + 128 * (q + 1)],
                            w2lT[:, 1024 * hc + 512 * n:
                                 1024 * hc + 512 * (n + 1)],
                            start=(hc == 0), stop=False)
                tensor.ldweights(onesr_bf[:])
                for n in range(2):
                    ins = tensor.matmul(ps_lo[:, 512 * n:512 * (n + 1)],
                                        onesr_bf[:],
                                        b2l_r[0:1, 512 * n:512 * (n + 1)],
                                        start=False, stop=True)
                inc("te", ins)
                ev[("lo", st, q)] = V["te"]

            @blk.vector
            def _(vector, st=st, q=q):
                w(vector, "se")
                w(vector, "te", ev[("lo", st, q)])
                if ("loW", st, q - 2) in ev:
                    w(vector, "lW", ev[("loW", st, q - 2)])
                elif ("loW", st - 1, 2 + q) in ev:
                    w(vector, "lW", ev[("loW", st - 1, 2 + q)])
                inc("ve", vector.tensor_copy(out=lob[q % 2][:], in_=ps_lo[:]))
                ev[("loE", st, q)] = V["ve"]

            if li_mode >= 4:
                @blk.gpsimd
                def _(gpsimd, st=st, q=q):
                    t = 4 * st + q
                    w(gpsimd, "ve", ev[("loE", st, q)])
                    inc("lW", gpsimd.dma_start(
                        out=light_d[128 * t:128 * (t + 1), :],
                        in_=lob[q % 2][:]), 16)
                    ev[("loW", st, q)] = V["lW"]

    li.close()

    if upto <= 5:
        return _finish([xnsc])

    if dumps:
        @blk.sync
        def _(sync):
            w(sync, "lW", 16)
            inc("wD", sync.dma_start(out=dbg["d_light_t0"],
                                     in_=light_d[0:128, :]), 16)
            w(sync, "se", ev[("xn2f", 0)])
            inc("wD", sync.dma_start(out=dbg["d_xnf_t0"],
                                     in_=xn2f[:, 0:DM]), 16)

    _barrier()

    cd = contextlib.ExitStack()
    srows = cd.enter_context(nc.sbuf_tensor("srows", [128, 8 * 32], F32))
    ps_tot = cd.enter_context(nc.psum_tensor("ps_tot", [1, 1], F32))
    ps_ar = cd.enter_context(nc.psum_tensor("ps_ar", [128, 1], F32))
    ps_pp = cd.enter_context(nc.psum_tensor("ps_pp", [128, 64], F32))
    ps_cs = cd.enter_context(nc.psum_tensor("ps_cs", [1, 64], F32))
    ps_pb = cd.enter_context(nc.psum_tensor("ps_pb", [128, 64], F32))
    ps_r0 = cd.enter_context(nc.psum_tensor("ps_r0", [128, 1], F32))

    @blk.sync
    def _(sync):
        w(sync, "ve", ev[("scol", NT - 1)])
        inc("sD", sync.dma_start(
            out=sag_in[0:1, :].rearrange("q (r t) -> (q r) t", r=128),
            in_=s_mine[:]), 16)
        ev["sagW"] = V["sD"]

    @blk.gpsimd
    def _(gpsimd):
        w(gpsimd, "sD", ev["sagW"])
        gpsimd.collective_compute(
            "AllGather", ALU.bypass, replica_groups=RG8,
            ins=[sag_in[:]], outs=[sag_out[:]],
        ).then_inc(S["cc2"], 1)
        gpsimd.wait_ge(S["cc2"], 1)
    V["cc2"] = 1

    @blk.sync
    def _(sync):
        w(sync, "cc2", 1)
        w(sync, "lW", 16 * NT)
        for r in range(8):
            inc("sD", sync.dma_start(
                out=srows[:, 32 * r:32 * (r + 1)],
                in_=sag_out[r:r + 1, :].rearrange("q (r2 t) -> (q r2) t",
                                                  r2=128)), 16)
        ev["srl"] = V["sD"]

    @blk.vector
    def _(vector):
        w(vector, "sD", ev["srl"])
        inc("ve", vector.memset(s_full[:], 0.0))
        for r in range(8):
            w(vector, "ve")
            inc("ve", vector.scalar_tensor_tensor(
                out=s_full[:, 0:32], in0=srows[:, 32 * r:32 * (r + 1)],
                scalar=ssel_t[:, r:r + 1], in1=s_full[:, 0:32],
                op0=ALU.mult, op1=ALU.add))
            inc("ve", vector.scalar_tensor_tensor(
                out=s_full[:, 32:64], in0=srows[:, 32 * r:32 * (r + 1)],
                scalar=ssel_t[:, 8 + r:9 + r], in1=s_full[:, 32:64],
                op0=ALU.mult, op1=ALU.add))
        inc("ve", vector.memset(a_rep[:], A1))
        inc("ve", vector.memset(na_rep[:], -A1))
        inc("ve", vector.memset(a_sc[:], A1))
        ev["sfull"] = V["ve"]

    for it in range(NITER):
        @blk.scalar
        def _(scalar, it=it):
            if it == 0:
                w(scalar, "ve", ev["sfull"])
            inc("se", scalar.activation(out=u64[:], in_=s_full[:],
                                        func=AF.Relu, scale=-1.0,
                                        bias=na_rep[:]))
            w(scalar, "se")
            inc("se", scalar.activation(out=u64b[:], in_=u64[:], func=AF.Exp,
                                        scale=-1.0, accum_out=part1[:]))
            ev[("cde", it)] = V["se"]

        @blk.tensor
        def _(tensor, it=it):
            w(tensor, "se", ev[("cde", it)])
            inc("te", tensor.matmul(ps_tot[:], onesc_f[:], part1[:],
                                    start=True, stop=True))
            ev[("cdt", it)] = V["te"]

        @blk.scalar
        def _(scalar, it=it):
            w(scalar, "te", ev[("cdt", it)])
            inc("se", scalar.activation(out=lntot[:], in_=ps_tot[:],
                                        func=AF.Ln))
            ev[("cdl", it)] = V["se"]

        @blk.tensor
        def _(tensor, it=it):
            w(tensor, "se", ev[("cdl", it)])
            tensor.matmul(ps_ar[:], negr_f[:], lntot[:], start=True, stop=False)
            tensor.matmul(ps_ar[:], onesr_f[:], logk_t[:], start=False,
                          stop=False)
            inc("te", tensor.matmul(ps_ar[:], onesr_f[:], a_sc[:], start=False,
                                    stop=True))
            ev[("cdb", it)] = V["te"]

        @blk.scalar
        def _(scalar, it=it):
            w(scalar, "te", ev[("cdb", it)])
            inc("se", scalar.copy(out=a_rep[:], in_=ps_ar[:]))
            inc("se", scalar.activation(out=na_rep[:], in_=ps_ar[:],
                                        func=AF.Copy, scale=-1.0))
            inc("se", scalar.copy(out=a_sc[:], in_=ps_ar[0:1, :]))
            w(scalar, "se")
            ev[("cdc", it)] = V["se"]

    @blk.vector
    def _(vector):
        w(vector, "se", ev[("cdc", NITER - 1)])
        inc("ve", vector.tensor_scalar(out=flags_f[:], in0=s_full[:],
                                       scalar1=a_rep[:], scalar2=0.0,
                                       op0=ALU.add, op1=ALU.is_ge))
        w(vector, "ve")
        inc("ve", vector.tensor_copy(out=flags_b[:], in_=flags_f[:]))
        ev["flags"] = V["ve"]

    @blk.tensor
    def _(tensor):
        w(tensor, "ve", ev["flags"])
        tensor.ldweights(triu[:])
        inc("te", tensor.matmul(ps_pp[:], triu[:], flags_b[:], start=True,
                                stop=True))
        inc("te", tensor.matmul(ps_cs[:], onesc_f[:], flags_f[:], start=True,
                                stop=True))
        ev["ppmm"] = V["te"]

    @blk.vector
    def _(vector):
        w(vector, "te", ev["ppmm"])
        inc("ve", vector.tensor_copy(out=pp_sb[:], in_=ps_pp[:]))
        inc("ve", vector.tensor_copy(out=cs_sb[:], in_=ps_cs[:]))
        w(vector, "ve")
        inc("ve", vector.tensor_copy(out=ipA[:], in_=cs_sb[:]))
        cur, nxt = ipA, ipB
        for k in [1, 2, 4, 8, 16, 32]:
            w(vector, "ve")
            inc("ve", vector.tensor_copy(out=nxt[0:1, 0:k], in_=cur[0:1, 0:k]))
            inc("ve", vector.tensor_tensor(out=nxt[0:1, k:64],
                                           in0=cur[0:1, k:64],
                                           in1=cur[0:1, 0:64 - k], op=ALU.add))
            cur, nxt = nxt, cur
        w(vector, "ve")
        inc("ve", vector.memset(coloff[0:1, 0:1], 0.0))
        inc("ve", vector.tensor_copy(out=coloff[0:1, 1:64], in_=cur[0:1, 0:63]))
        ev["coloff"] = V["ve"]

    @blk.tensor
    def _(tensor):
        w(tensor, "ve", ev["coloff"])
        inc("te", tensor.matmul(ps_pb[:], onesr_f[:], coloff[:], start=True,
                                stop=True))
        inc("te", tensor.matmul(ps_r0[:], onesr_f[:], coloff[0:1, 32:33],
                                start=True, stop=True))
        ev["cobc"] = V["te"]

    @blk.vector
    def _(vector):
        w(vector, "te", ev["cobc"])
        inc("ve", vector.scalar_tensor_tensor(
            out=rank_sb[:], in0=ps_pb[:], scalar=1.0, in1=pp_sb[:],
            op0=ALU.mult, op1=ALU.add))
        inc("ve", vector.tensor_copy(out=r0t[:], in_=ps_r0[:]))
        w(vector, "ve")
        inc("ve", vector.tensor_tensor(out=r0t[:], in0=r0t[:],
                                       in1=hsel_t[:, 0:1], op=ALU.mult))
        w(vector, "ve")
        inc("ve", vector.tensor_scalar(out=sc64a[:], in0=rank_sb[:],
                                       scalar1=r0t[:], scalar2=None,
                                       op0=ALU.subtract))
        inc("ve", vector.tensor_single_scalar(out=sc64b[:], in_=rank_sb[:],
                                              scalar=float(CAP) + 0.5,
                                              op=ALU.is_le))
        w(vector, "ve")
        inc("ve", vector.tensor_tensor(out=sc64b[:], in0=sc64b[:],
                                       in1=flags_f[:], op=ALU.mult))
        inc("ve", vector.tensor_scalar(out=sc64a[:], in0=sc64a[:],
                                       scalar1=capr1[:], scalar2=None,
                                       op0=ALU.subtract))
        w(vector, "ve")
        inc("ve", vector.tensor_tensor(out=sc64a[:], in0=sc64a[:],
                                       in1=sc64b[:], op=ALU.mult))
        w(vector, "ve")
        inc("ve", vector.tensor_scalar(out=sc64a[:], in0=sc64a[:],
                                       scalar1=capr[:], scalar2=None,
                                       op0=ALU.add))
        w(vector, "ve")
        inc("ve", vector.tensor_scalar(out=sc64b[:, 0:32],
                                       in0=sc64a[:, 0:32],
                                       scalar1=hsel_t[:, 1:2], scalar2=None,
                                       op0=ALU.mult))
        w(vector, "ve")
        inc("ve", vector.scalar_tensor_tensor(
            out=sc64b[:, 0:32], in0=sc64a[:, 32:64], scalar=hsel_t[:, 0:1],
            in1=sc64b[:, 0:32], op0=ALU.mult, op1=ALU.add))
        w(vector, "ve")
        inc("ve", vector.tensor_copy(out=slots_i[:], in_=sc64b[:, 0:32]))
        ev["slots"] = V["ve"]

    if dumps:
        @blk.sync
        def _(sync):
            w(sync, "ve", ev["slots"])
            inc("wD", sync.dma_start(out=dbg["d_sfull"], in_=s_full[:]), 16)
            inc("wD", sync.dma_start(out=dbg["d_arep"], in_=a_rep[:]), 16)
            inc("wD", sync.dma_start(out=dbg["d_rank"], in_=rank_sb[:]), 16)
            inc("wD", sync.dma_start(out=dbg["d_slots"],
                                     in_=sc64b[:, 0:32]), 16)

    cd.close()

    if upto <= 6:
        return _finish([xnsc])

    @blk.gpsimd
    def _(gpsimd):
        for g in range(9):
            inc("cinZ", gpsimd.dma_start(
                out=cin_d[128 * g:128 * (g + 1), :], in_=zero_bf[:]), 16)
        inc("coutZ", gpsimd.dma_start(
            out=cout_d[CAP:CAP + 128, :], in_=zero_f[:]), 16)
        w(gpsimd, "ve", ev["slots"])
        w(gpsimd, "cinZ", V["cinZ"])
        for t in range(NT):
            inc("cinW", gpsimd.indirect_dma_start(
                out=cin_d[:],
                out_offset=IndirectOffsetOnAxis(slots_i[:, t:t + 1], 0),
                in_=xn2f[:, DM * t:DM * (t + 1)],
                in_offset=None), 16)
        ev["scat"] = V["cinW"]

    xnsc.close()

    if upto <= 7:
        return _finish([])

    _barrier()

    p4 = contextlib.ExitStack()
    w2h_sb = p4.enter_context(nc.sbuf_tensor("w2h_sb", [128, 32 * DM], BF16))
    xgT = p4.enter_context(nc.sbuf_tensor("xgT", [128, 8 * 512], BF16))
    h1hT = p4.enter_context(nc.sbuf_tensor("h1hT", [128, 32 * 512], BF16))
    whb = [p4.enter_context(nc.sbuf_tensor(f"whb_{i}", [128, 8 * 128], BF16))
           for i in range(2)]
    cing = [p4.enter_context(nc.sbuf_tensor(f"cing_{i}", [128, DM], BF16))
            for i in range(2)]
    hov = p4.enter_context(nc.sbuf_tensor("hov", [128, 512], F32))
    ps_xp4 = p4.enter_context(nc.psum_tensor("ps_xp4", [128, 1024], BF16))
    ps_h4 = [p4.enter_context(nc.psum_tensor(f"ps_h4_{i}", [128, 512], F32))
             for i in range(2)]
    ps_w2 = p4.enter_context(nc.psum_tensor("ps_w2", [128, 2048], F32))

    @blk.sync
    def _(sync):
        w(sync, "hS", ev["hconv"])
        w(sync, "cinW", ev["scat"])
        w(sync, "lW", 16 * NT)
        for hc in range(32):
            inc("wD", sync.dma_start(
                out=w2h_sb[:, DM * hc:DM * (hc + 1)],
                in_=w2hT_d[128 * hc:128 * (hc + 1), :]), 16)
        ev["w2hl"] = V["wD"]

    for cc in range(2):
        for g in range(4):
            @blk.sync
            def _(sync, cc=cc, g=g):
                w(sync, "cinW", ev["scat"])
                if ("xgE", cc, g - 2) in ev:
                    w(sync, "se", ev[("xgE", cc, g - 2)])
                elif cc > 0:
                    w(sync, "se", ev[("xgE", cc - 1, 2 + g)])
                inc("cinR", sync.dma_start(
                    out=cing[g % 2][:],
                    in_=cin_d[512 * cc + 128 * g:512 * cc + 128 * (g + 1), :]),
                    16)
                ev[("cinl", cc, g)] = V["cinR"]

            @blk.tensor
            def _(tensor, cc=cc, g=g):
                w(tensor, "cinR", ev[("cinl", cc, g)])
                if ("xgE", cc, g - 1) in ev:
                    w(tensor, "se", ev[("xgE", cc, g - 1)])
                elif cc > 0:
                    w(tensor, "se", ev[("xgE", cc - 1, 3)])
                else:
                    w(tensor, "ve", ev["slots"])
                ins = None
                for kc in range(8):
                    ins = tensor.transpose(
                        ps_xp4[:, 128 * kc:128 * (kc + 1)],
                        cing[g % 2][:, 128 * kc:128 * (kc + 1)], ident[:])
                inc("te", ins)
                ev[("xg", cc, g)] = V["te"]

            @blk.scalar
            def _(scalar, cc=cc, g=g):
                w(scalar, "te", ev[("xg", cc, g)])
                dst = xgT[:].rearrange("p (kc j) -> p kc j", kc=8)
                inc("se", scalar.copy(
                    out=dst[:, :, 128 * g:128 * (g + 1)],
                    in_=ps_xp4[:].rearrange("p (kc j) -> p kc j", kc=8)))
                ev[("xgE", cc, g)] = V["se"]

        for hc in range(32):
            @blk.sync
            def _(sync, cc=cc, hc=hc):
                if ("h1mm", cc, hc - 2) in ev:
                    w(sync, "te", ev[("h1mm", cc, hc - 2)])
                elif cc > 0:
                    w(sync, "te", ev[("h1mm", cc - 1, 30 + hc)])
                w(sync, "hS", ev["hconv"])
                w(sync, "cinW", ev["scat"])
                src = w1hT_d.rearrange("(kc p) h2 -> p kc h2", p=128)
                inc("hD", sync.dma_start(
                    out=whb[hc % 2][:].rearrange("p (kc j) -> p kc j", kc=8),
                    in_=src[:, :, 128 * hc:128 * (hc + 1)]), 16)
                ev[("whl", cc, hc)] = V["hD"]

            @blk.tensor
            def _(tensor, cc=cc, hc=hc):
                w(tensor, "hD", ev[("whl", cc, hc)])
                w(tensor, "se", ev[("xgE", cc, 3)])
                if ("h1e", cc, hc - 2) in ev:
                    w(tensor, "se", ev[("h1e", cc, hc - 2)])
                elif cc > 0:
                    w(tensor, "se", ev[("h1e", cc - 1, 30 + hc)])
                ins = None
                for kc in range(8):
                    tensor.ldweights(whb[hc % 2][:, 128 * kc:128 * (kc + 1)])
                    ins = tensor.matmul(
                        ps_h4[hc % 2][:],
                        whb[hc % 2][:, 128 * kc:128 * (kc + 1)],
                        xgT[:, 512 * kc:512 * (kc + 1)],
                        start=(kc == 0), stop=(kc == 7))
                inc("te", ins)
                ev[("h1mm", cc, hc)] = V["te"]

            @blk.scalar
            def _(scalar, cc=cc, hc=hc):
                w(scalar, "te", ev[("h1mm", cc, hc)])
                inc("se", scalar.activation(
                    out=h1hT[:, 512 * hc:512 * (hc + 1)], in_=ps_h4[hc % 2][:],
                    func=AF.Gelu, bias=b1h_t[:, hc:hc + 1]))
                ev[("h1e", cc, hc)] = V["se"]

        for n in range(2):
            @blk.tensor
            def _(tensor, cc=cc, n=n):
                w(tensor, "se", ev[("h1e", cc, 31)])
                w(tensor, "wD", ev["w2hl"])
                if ("hovW", cc, n - 1, 3) in ev:
                    w(tensor, "coutW", ev[("hovW", cc, n - 1, 3)])
                elif cc > 0:
                    w(tensor, "coutW", ev[("hovW", cc - 1, 1, 3)])
                ins = None
                for hc in range(32):
                    for g in range(4):
                        tensor.ldweights(
                            h1hT[:, 512 * hc + 128 * g:512 * hc + 128 * (g + 1)])
                        ins = tensor.matmul(
                            ps_w2[:, 512 * g:512 * (g + 1)],
                            h1hT[:, 512 * hc + 128 * g:512 * hc + 128 * (g + 1)],
                            w2h_sb[:, DM * hc + 512 * n:DM * hc + 512 * (n + 1)],
                            start=(hc == 0), stop=False)
                tensor.ldweights(onesr_bf[:])
                for g in range(4):
                    ins = tensor.matmul(
                        ps_w2[:, 512 * g:512 * (g + 1)], onesr_bf[:],
                        b2h_r[0:1, 512 * n:512 * (n + 1)],
                        start=False, stop=True)
                inc("te", ins)
                ev[("w2mm", cc, n)] = V["te"]

            for g in range(4):
                @blk.vector
                def _(vector, cc=cc, n=n, g=g):
                    w(vector, "te", ev[("w2mm", cc, n)])
                    if ("hovW", cc, n, g - 1) in ev:
                        w(vector, "coutW", ev[("hovW", cc, n, g - 1)])
                    inc("ve", vector.tensor_copy(
                        out=hov[:], in_=ps_w2[:, 512 * g:512 * (g + 1)]))
                    ev[("hovE", cc, n, g)] = V["ve"]

                @blk.gpsimd
                def _(gpsimd, cc=cc, n=n, g=g):
                    w(gpsimd, "ve", ev[("hovE", cc, n, g)])
                    inc("coutW", gpsimd.dma_start(
                        out=cout_d[512 * cc + 128 * g:512 * cc + 128 * (g + 1),
                                   512 * n:512 * (n + 1)],
                        in_=hov[:]), 16)
                    ev[("hovW", cc, n, g)] = V["coutW"]

    p4.close()

    if upto <= 8:
        return _finish([])

    _barrier()

    p5 = contextlib.ExitStack()
    lig = [p5.enter_context(nc.sbuf_tensor(f"lig_{i}", [128, DM], F32))
           for i in range(2)]
    x2r = [p5.enter_context(nc.sbuf_tensor(f"x2r_{i}", [128, DM], F32))
           for i in range(2)]
    hvb = [p5.enter_context(nc.sbuf_tensor(f"hv_{i}", [128, DM], F32))
           for i in range(2)]
    zb = [p5.enter_context(nc.sbuf_tensor(f"zb_{i}", [128, DM], F32))
          for i in range(2)]
    t3b = p5.enter_context(nc.sbuf_tensor("t3b", [128, DM], F32))
    ob = [p5.enter_context(nc.sbuf_tensor(f"ob_{i}", [128, DM], F32))
          for i in range(2)]

    for t in range(NT):
        p = t % 2

        @blk.sync
        def _(sync, t=t, p=p):
            if t == 0:
                w(sync, "lW", 16 * NT)
                w(sync, "x2W", 16 * NT)
                w(sync, "coutW", V["coutW"])
            if ("zf", t - 2) in ev:
                w(sync, "ve", ev[("zf", t - 2)])
            if ("ofin", t - 2) in ev:
                w(sync, "ve", ev[("ofin", t - 2)])
            inc("lR", sync.dma_start(out=lig[p][:],
                                     in_=light_d[128 * t:128 * (t + 1), :]), 16)
            inc("x2R", sync.dma_start(out=x2r[p][:],
                                      in_=x2_d[128 * t:128 * (t + 1), :]), 16)
            ev[("p5l", t)] = (V["lR"], V["x2R"])

        @blk.gpsimd
        def _(gpsimd, t=t, p=p):
            if t == 0:
                w(gpsimd, "coutW", V["coutW"])
                w(gpsimd, "coutZ", V["coutZ"])
            if ("zf", t - 2) in ev:
                w(gpsimd, "ve", ev[("zf", t - 2)])
            inc("hvR", gpsimd.indirect_dma_start(
                out=hvb[p][:], out_offset=None,
                in_=cout_d[:],
                in_offset=IndirectOffsetOnAxis(slots_i[:, t:t + 1], 0)), 16)
            ev[("hvl", t)] = V["hvR"]

        @blk.vector
        def _(vector, t=t, p=p):
            w(vector, "lR", ev[("p5l", t)][0])
            w(vector, "hvR", ev[("hvl", t)])
            inc("ve", vector.scalar_tensor_tensor(
                out=zb[p][:], in0=lig[p][:], scalar=1.0, in1=hvb[p][:],
                op0=ALU.mult, op1=ALU.add, accum_out=t1c[:]))
            ev[("zf", t)] = V["ve"]

        @blk.scalar
        def _(scalar, t=t, p=p):
            w(scalar, "ve", ev[("zf", t)])
            inc("se", scalar.activation(out=junk_s[:], in_=zb[p][:],
                                        func=AF.Square, accum_out=t2c[:]))
            ev[("z2", t)] = V["se"]

        @blk.vector
        def _(vector, t=t, p=p):
            w(vector, "se", ev[("z2", t)])
            w(vector, "ve")
            inc("ve", vector.tensor_scalar(out=t1c[:], in0=t1c[:],
                                           scalar1=1.0 / DM, scalar2=None,
                                           op0=ALU.mult))
            w(vector, "ve")
            inc("ve", vector.tensor_tensor(out=t3c[:], in0=t1c[:], in1=t1c[:],
                                           op=ALU.mult))
            inc("ve", vector.tensor_scalar(out=t2c[:], in0=t2c[:],
                                           scalar1=1.0 / DM, scalar2=None,
                                           op0=ALU.mult))
            w(vector, "ve")
            inc("ve", vector.tensor_tensor(out=t2c[:], in0=t2c[:], in1=t3c[:],
                                           op=ALU.subtract))
            ev[("v3", t)] = V["ve"]

        @blk.scalar
        def _(scalar, t=t, p=p):
            w(scalar, "ve", ev[("v3", t)])
            inc("se", scalar.activation(out=t3c[:], in_=t2c[:], func=AF.Sqrt,
                                        bias=eps5[:]))
            ev[("sq3", t)] = V["se"]

        @blk.vector
        def _(vector, t=t, p=p):
            w(vector, "se", ev[("sq3", t)])
            inc("ve", vector.reciprocal(out=t3c[:], in_=t3c[:]))
            w(vector, "ve")
            inc("ve", vector.scalar_tensor_tensor(
                out=t4c[:], in0=t1c[:], scalar=-1.0, in1=t3c[:],
                op0=ALU.mult, op1=ALU.mult))
            ev[("r3", t)] = V["ve"]

        @blk.scalar
        def _(scalar, t=t, p=p):
            w(scalar, "ve", ev[("r3", t)])
            inc("se", scalar.activation(out=t3b[:], in_=zb[p][:], func=AF.Identity,
                                        scale=t3c[:], bias=t4c[:]))
            ev[("t3n", t)] = V["se"]

        @blk.vector
        def _(vector, t=t, p=p):
            w(vector, "se", ev[("t3n", t)])
            w(vector, "x2R", ev[("p5l", t)][1])
            inc("ve", vector.tensor_tensor(out=ob[p][:], in0=x2r[p][:],
                                           in1=t3b[:], op=ALU.add))
            ev[("ofin", t)] = V["ve"]

        @blk.sync
        def _(sync, t=t, p=p):
            w(sync, "ve", ev[("ofin", t)])
            inc("outW", sync.dma_start(out=out_h[128 * t:128 * (t + 1), :],
                                       in_=ob[p][:]), 16)

    @blk.sync
    def _(sync):
        w(sync, "outW", 16 * NT)

    p5.close()
    ctx.close()
    return nc, dbg


_CACHE = {}


def _build_in_maps(x, wts):
    in_maps = []
    for c in range(8):
        b, h = c // 2, c % 2
        m = dict(wts)
        m["x"] = np.ascontiguousarray(x[b, 4096 * h:4096 * (h + 1), :])
        kvsel = np.zeros((64, 8), np.float32)
        kvsel[:, 2 * b] = 1.0
        kvsel[:, 2 * b + 1] = 1.0
        m["kvsel"] = kvsel
        ssel = np.zeros((128, 16), np.float32)
        ssel[:, 2 * b] = 1.0
        ssel[:, 8 + 2 * b + 1] = 1.0
        m["ssel"] = ssel
        hsel = np.zeros((128, 2), np.float32)
        hsel[:, 0] = float(h)
        hsel[:, 1] = 1.0 - float(h)
        m["hsel"] = hsel
        in_maps.append(m)
    return in_maps


def _pack_weights(L0_attn_Wqkv, L0_attn_Wg, L0_attn_bg, L0_attn_Wp, L0_attn_bp,
                  L0_ffn_routing_token, L0_ffn_gamma_l, L0_ffn_w1_l,
                  L0_ffn_b1_l, L0_ffn_w2_l, L0_ffn_b2_l, L0_ffn_gamma_h,
                  L0_ffn_w1_h, L0_ffn_b1_h, L0_ffn_w2_h, L0_ffn_b2_h):
    f = np.float32
    return {
        "Wqkv": np.ascontiguousarray(np.asarray(L0_attn_Wqkv, f)),
        "Wg": np.ascontiguousarray(np.asarray(L0_attn_Wg, f)),
        "bg": np.asarray(L0_attn_bg, f).reshape(1, DM),
        "Wp": np.ascontiguousarray(np.asarray(L0_attn_Wp, f)),
        "bp": np.asarray(L0_attn_bp, f).reshape(1, DM),
        "rt": np.asarray(L0_ffn_routing_token, f).reshape(1, DM),
        "gl": np.asarray(L0_ffn_gamma_l, f).reshape(1, DM),
        "w1l": np.ascontiguousarray(np.asarray(L0_ffn_w1_l, f)),
        "b1l": np.asarray(L0_ffn_b1_l, f).reshape(1, 512),
        "w2l": np.ascontiguousarray(np.asarray(L0_ffn_w2_l, f)),
        "b2l": np.asarray(L0_ffn_b2_l, f).reshape(1, DM),
        "gh": np.asarray(L0_ffn_gamma_h, f).reshape(1, DM),
        "w1h": np.ascontiguousarray(np.asarray(L0_ffn_w1_h, f)),
        "b1h": np.asarray(L0_ffn_b1_h, f).reshape(1, 4096),
        "w2h": np.ascontiguousarray(np.asarray(L0_ffn_w2_h, f)),
        "b2h": np.asarray(L0_ffn_b2_h, f).reshape(1, DM),
    }


def kernel(L0_attn_x, **kw):
    if "nc" not in _CACHE:
        _CACHE["nc"] = build(debug=True)[0]
    nc = _CACHE["nc"]
    x = np.ascontiguousarray(np.asarray(L0_attn_x, np.float32))
    wts = _pack_weights(**kw)
    in_maps = _build_in_maps(x, wts)
    res = run_bass_kernel_spmd(nc, in_maps, core_ids=list(range(8)))
    out = np.empty((4, 8192, DM), np.float32)
    for c in range(8):
        b, h = c // 2, c % 2
        out[b, 4096 * h:4096 * (h + 1), :] = np.asarray(res.results[c]["out"])
    return out
